# version 02 (final)
# baseline (speedup 1.0000x reference)
import time
import contextlib
import numpy as np
import ml_dtypes
import concourse.bass as bass
import concourse.mybir as mybir
from concourse.bass_utils import run_bass_kernel_spmd
from concourse.expressions import s_not_equal, s_logical_and
from concourse.expressions_rust import make_scalar_value

F32 = mybir.dt.float32
BF16 = mybir.dt.bfloat16
I32 = mybir.dt.int32
AF = mybir.ActivationFunctionType
ALU = mybir.AluOpType

B, T, C, H, D = 4, 8192, 1024, 16, 64
HPC, CPC, NSG, NCG = 8, 512, 4, 8
L = 32
LP = L + 1
TILE = 256
NT = T // TILE
KPT = TILE // L
EPS = 1e-5
E_HALF = float(np.exp(0.5))
E_MHALF = float(np.exp(-0.5))
TOK_OWN = T // 2
NBLK = 4
FSPAN = 512
NSPAN = TOK_OWN // FSPAN
FF = 4096
NE = 8

BUILD_TILES = NT
BUILD_PHASES = 3
STAGE_LIMIT = 99
FFN_LIMIT = 99
DEBUG_TAPS = []


class SemT:
    def __init__(self, nc, name):
        self.h = nc.alloc_semaphore(name)
        self.n = 0

    def inc(self, instr, k=1):
        instr.then_inc(self.h, k)
        self.n += k

    def dinc(self, instr):
        instr.then_inc(self.h, 16)
        self.n += 16

    def wait(self, eng):
        eng.wait_ge(self.h, self.n)


def build_program():
    nc = bass.Bass("TRN2", target_bir_lowering=False, detect_race_conditions=False)
    ap = lambda h: h.ap()

    x_cm = ap(nc.dram_tensor("x_cm", [C, T], F32, kind="ExternalInput"))
    x_own = ap(nc.dram_tensor("x_own", [C, TOK_OWN], F32, kind="ExternalInput"))
    w_r = ap(nc.dram_tensor("w_r", [C, C], F32, kind="ExternalInput"))
    w_k = ap(nc.dram_tensor("w_k", [C, CPC], F32, kind="ExternalInput"))
    w_v = ap(nc.dram_tensor("w_v", [C, CPC], F32, kind="ExternalInput"))
    wo_own = ap(nc.dram_tensor("wo_own", [CPC, C], F32, kind="ExternalInput"))
    wo_par = ap(nc.dram_tensor("wo_par", [CPC, C], F32, kind="ExternalInput"))
    wla = ap(nc.dram_tensor("wla", [C, 64], F32, kind="ExternalInput"))
    wlb = ap(nc.dram_tensor("wlb", [64, CPC], F32, kind="ExternalInput"))
    wlbias = ap(nc.dram_tensor("wlbias", [128, NSG], F32, kind="ExternalInput"))
    ala = ap(nc.dram_tensor("ala", [C, 64], F32, kind="ExternalInput"))
    alb = ap(nc.dram_tensor("alb", [64, CPC], F32, kind="ExternalInput"))
    albias = ap(nc.dram_tensor("albias", [128, NSG], F32, kind="ExternalInput"))
    gla = ap(nc.dram_tensor("gla", [C, 128], F32, kind="ExternalInput"))
    glb = ap(nc.dram_tensor("glb", [128, CPC], F32, kind="ExternalInput"))
    gbias = ap(nc.dram_tensor("gbias", [128, NSG], F32, kind="ExternalInput"))
    xmix = ap(nc.dram_tensor("xmix", [128, 5 * NCG], F32, kind="ExternalInput"))
    kkv = ap(nc.dram_tensor("kkv", [128, NSG], F32, kind="ExternalInput"))
    kav = ap(nc.dram_tensor("kav", [128, NSG], F32, kind="ExternalInput"))
    rkv = ap(nc.dram_tensor("rkv", [128, NSG], F32, kind="ExternalInput"))
    gnw = ap(nc.dram_tensor("gnw", [128, NSG], F32, kind="ExternalInput"))
    gnb = ap(nc.dram_tensor("gnb", [128, NSG], F32, kind="ExternalInput"))
    wgt = ap(nc.dram_tensor("wgt", [C, 9], F32, kind="ExternalInput"))
    btv = ap(nc.dram_tensor("btv", [128, 1], F32, kind="ExternalInput"))
    bgv = ap(nc.dram_tensor("bgv", [128, 8], F32, kind="ExternalInput"))
    ffw1 = ap(nc.dram_tensor("ffw1", [NE, C, FF], F32, kind="ExternalInput"))
    ffb1 = ap(nc.dram_tensor("ffb1", [NE, FF], F32, kind="ExternalInput"))
    ffw2 = ap(nc.dram_tensor("ffw2", [NE, FF, C], F32, kind="ExternalInput"))
    ffb2 = ap(nc.dram_tensor("ffb2", [NE, C], F32, kind="ExternalInput"))
    ones128 = ap(nc.dram_tensor("ones128", [128, 128], F32, kind="ExternalInput"))
    headones = ap(nc.dram_tensor("headones", [128, 128], F32, kind="ExternalInput"))
    ident = ap(nc.dram_tensor("ident", [128, 128], BF16, kind="ExternalInput"))
    identf = ap(nc.dram_tensor("identf", [128, 128], F32, kind="ExternalInput"))
    m_sl = ap(nc.dram_tensor("m_sl", [128, 256], F32, kind="ExternalInput"))
    m_su = ap(nc.dram_tensor("m_su", [128, 256], F32, kind="ExternalInput"))
    m_ui = ap(nc.dram_tensor("m_ui", [128, 256], F32, kind="ExternalInput"))
    i64rep = ap(nc.dram_tensor("i64rep", [128, 256], BF16, kind="ExternalInput"))
    onesrow = ap(nc.dram_tensor("onesrow", [1, 512], BF16, kind="ExternalInput"))
    sel_e = ap(nc.dram_tensor("sel_e", [8, NE * 128], F32, kind="ExternalInput"))
    epsc = ap(nc.dram_tensor("epsc", [128, 1], F32, kind="ExternalInput"))
    nonces = ap(nc.dram_tensor("nonces", [NBLK, 64], I32, kind="ExternalInput"))
    sh_og = ap(nc.dram_tensor("sh_og", [2, NBLK, 128 * NSG * 1024], BF16, kind="Internal",
                              addr_space="Shared"))
    sh_flag = ap(nc.dram_tensor("sh_flag", [2, NBLK, 64], I32, kind="Internal",
                                addr_space="Shared"))
    og_all = ap(nc.dram_tensor("og_all", [2 * NBLK, 128 * NSG * 1024], BF16, kind="Internal"))
    ogp_dram = ap(nc.dram_tensor("ogp_dram", [1, 128 * NSG * 1024], BF16, kind="Internal"))
    ogo_dram = ap(nc.dram_tensor("ogo_dram", [1, 128 * NSG * 1024], BF16, kind="Internal"))
    dummy_dram = ap(nc.dram_tensor("dummy_dram", [1, 64], I32, kind="Internal"))
    x1_sp = ap(nc.dram_tensor("x1_sp", [128, NCG * TOK_OWN], F32, kind="Internal"))
    y_out = ap(nc.dram_tensor("y_out", [C, TOK_OWN], F32, kind="ExternalOutput"))

    tsr, vec, act, pool, syn = nc.tensor, nc.vector, nc.scalar, nc.gpsimd, nc.sync

    s_in = SemT(nc, "s_in")
    s_wt = SemT(nc, "s_wt")
    s_pe = SemT(nc, "s_pe")
    s_dv = SemT(nc, "s_dv")
    s_ac = SemT(nc, "s_ac")
    s_pl = SemT(nc, "s_pl")
    s_st = SemT(nc, "s_st")
    s_x2 = SemT(nc, "s_x2")
    s_xt = SemT(nc, "s_xt")
    s_b1 = SemT(nc, "s_b1")
    s_b2 = SemT(nc, "s_b2")
    s_b3 = SemT(nc, "s_b3")

    def tapdma(name, src_ap, rows, cols, dt=F32):
        if name not in DEBUG_TAPS:
            return
        for s in (s_pe, s_dv, s_ac):
            s.wait(syn)
        t = ap(nc.dram_tensor("tap_" + name, [rows, cols], dt, kind="ExternalOutput"))
        s_st.dinc(syn.dma_start(out=t[:, :], in_=src_ap))
        for e in (tsr, vec, act, pool):
            s_st.wait(e)

    def wait_all(sems, engines):
        for e in engines:
            for s in sems:
                s.wait(e)

    COMP = None

    with contextlib.ExitStack() as _es:
        c_ones = _es.enter_context(nc.sbuf_tensor("c_ones128", [128, 128], F32))
        c_ho = _es.enter_context(nc.sbuf_tensor("c_headones", [128, 128], F32))
        c_id = _es.enter_context(nc.sbuf_tensor("c_ident", [128, 128], BF16))
        c_idf = _es.enter_context(nc.sbuf_tensor("c_identf", [128, 128], F32))
        c_msl = _es.enter_context(nc.sbuf_tensor("c_msl", [128, 256], F32))
        c_msu = _es.enter_context(nc.sbuf_tensor("c_msu", [128, 256], F32))
        c_mui = _es.enter_context(nc.sbuf_tensor("c_mui", [128, 256], F32))
        c_i64 = _es.enter_context(nc.sbuf_tensor("c_i64", [128, 256], BF16))
        c_1row = _es.enter_context(nc.sbuf_tensor("c_onesrow", [1, 512], BF16))
        c_sel = _es.enter_context(nc.sbuf_tensor("c_sel", [8, NE * 128], F32))
        c_eps = _es.enter_context(nc.sbuf_tensor("c_eps", [128, 1], F32))
        v_xmix = _es.enter_context(nc.sbuf_tensor("v_xmix", [128, 5 * NCG], F32))
        v_kk = _es.enter_context(nc.sbuf_tensor("v_kk", [128, NSG], F32))
        v_ka = _es.enter_context(nc.sbuf_tensor("v_ka", [128, NSG], F32))
        v_1mka = _es.enter_context(nc.sbuf_tensor("v_1mka", [128, NSG], F32))
        v_rk = _es.enter_context(nc.sbuf_tensor("v_rk", [128, NSG], F32))
        v_gnw = _es.enter_context(nc.sbuf_tensor("v_gnw", [128, NSG], F32))
        v_gnb = _es.enter_context(nc.sbuf_tensor("v_gnb", [128, NSG], F32))
        v_wlbh = _es.enter_context(nc.sbuf_tensor("v_wlbh", [128, NSG], F32))
        v_wlbn = _es.enter_context(nc.sbuf_tensor("v_wlbn", [128, NSG], F32))
        v_albh = _es.enter_context(nc.sbuf_tensor("v_albh", [128, NSG], F32))
        v_gb = _es.enter_context(nc.sbuf_tensor("v_gb", [128, NSG], F32))
        v_bth = _es.enter_context(nc.sbuf_tensor("v_bth", [128, 1], F32))
        v_bg = _es.enter_context(nc.sbuf_tensor("v_bg", [128, 8], F32))
        v_nonce = _es.enter_context(nc.sbuf_tensor("v_nonce", [NBLK, 64], I32))
        wgt_sb = _es.enter_context(nc.sbuf_tensor("wgt_sb", [128, NCG * 9], BF16))
        scr0 = _es.enter_context(nc.sbuf_tensor("scr0", [128, NSG], F32))
        scr1 = _es.enter_context(nc.sbuf_tensor("scr1", [128, NSG], F32))
        for dst, src in [
            (c_ones[:, :], ones128[:, :]), (c_ho[:, :], headones[:, :]),
            (c_msl[:, :], m_sl[:, :]), (c_msu[:, :], m_su[:, :]),
            (c_mui[:, :], m_ui[:, :]),
            (c_sel[:, :], sel_e[:, :]), (c_eps[:, :], epsc[:, :]),
            (v_xmix[:, :], xmix[:, :]), (v_kk[:, :], kkv[:, :]),
            (v_ka[:, :], kav[:, :]), (v_rk[:, :], rkv[:, :]),
            (v_gnw[:, :], gnw[:, :]), (v_gnb[:, :], gnb[:, :]),
            (v_gb[:, :], gbias[:, :]), (v_nonce[:, :], nonces[:, :]),
            (v_bth[:, :], btv[:, :]), (scr0[:, :], wlbias[:, :]),
            (v_bg[:, :], bgv[:, :]),
            (scr1[:, :], albias[:, :]),
        ]:
            s_in.dinc(syn.dma_start(out=dst, in_=src))
        s_wt.dinc(pool.dma_start(out=c_id[:, :], in_=ident[:, :]))
        s_wt.dinc(pool.dma_start(out=c_idf[:, :], in_=identf[:, :]))
        s_wt.dinc(pool.dma_start(out=c_i64[:, :], in_=i64rep[:, :]))
        s_wt.dinc(pool.dma_start(out=c_1row[:, :], in_=onesrow[:, :]))
        s_wt.dinc(pool.dma_start(
            out=wgt_sb[:].rearrange("p (g n) -> p g n", n=9),
            in_=wgt.rearrange("(g p) n -> p g n", p=128)))
        s_in.wait(vec)
        vec.tensor_scalar(v_1mka[:, :], v_ka[:, :], -1.0, 1.0, ALU.mult, ALU.add)
        vec.tensor_scalar(v_wlbh[:, :], scr0[:, :], 0.5, None, ALU.mult)
        vec.tensor_scalar(v_wlbn[:, :], scr0[:, :], -1.0, None, ALU.mult)
        vec.tensor_scalar(v_bth[:, :], v_bth[:, :], 0.5, None, ALU.mult)
        ins = vec.tensor_scalar(v_albh[:, :], scr1[:, :], 0.5, None, ALU.mult)
        s_dv.inc(ins)

        pid_g = pool.partition_id()
        role_rg = pool.to_reg(pid_g % 2)

        with contextlib.ExitStack() as _es:
            wr_sb = _es.enter_context(nc.sbuf_tensor("wr_sb", [128, NCG * C], BF16))
            wk_sb = _es.enter_context(nc.sbuf_tensor("wk_sb", [128, NCG * CPC], BF16))
            wv_sb = _es.enter_context(nc.sbuf_tensor("wv_sb", [128, NCG * CPC], BF16))
            wla_sb = _es.enter_context(nc.sbuf_tensor("wla_sb", [128, NCG * 64], BF16))
            wlb_sb = _es.enter_context(nc.sbuf_tensor("wlb_sb", [64, CPC], BF16))
            ala_sb = _es.enter_context(nc.sbuf_tensor("ala_sb", [128, NCG * 64], BF16))
            alb_sb = _es.enter_context(nc.sbuf_tensor("alb_sb", [64, CPC], BF16))
            gla_sb = _es.enter_context(nc.sbuf_tensor("gla_sb", [128, NCG * 128], BF16))
            glb_sb = _es.enter_context(nc.sbuf_tensor("glb_sb", [128, CPC], BF16))
            st_sb = _es.enter_context(nc.sbuf_tensor("st_sb", [128, NSG * 128], F32))
            stbf_sb = _es.enter_context(nc.sbuf_tensor("stbf_sb", [128, NSG * 128], BF16))
            def wload(dst, src, width):
                s_wt.dinc(pool.dma_start(
                    out=dst.rearrange("p (k w) -> p k w", w=width),
                    in_=src.rearrange("(k p) w -> p k w", p=128)))

            wload(wr_sb[:], w_r, C)
            wload(wk_sb[:], w_k, CPC)
            wload(wv_sb[:], w_v, CPC)
            wload(wla_sb[:], wla, 64)
            wload(ala_sb[:], ala, 64)
            wload(gla_sb[:], gla, 128)
            s_wt.dinc(pool.dma_start(out=wlb_sb[:, :], in_=wlb[:, :]))
            s_wt.dinc(pool.dma_start(out=alb_sb[:, :], in_=alb[:, :]))
            s_wt.dinc(pool.dma_start(out=glb_sb[:, :], in_=glb[:, :]))

            with contextlib.ExitStack() as _es:
                x_sb = _es.enter_context(nc.sbuf_tensor("x_sb", [128, NCG * TILE], F32))
                xn_sb = _es.enter_context(nc.sbuf_tensor("xn_sb", [128, NCG * (TILE + 1)], F32))
                delta_sb = _es.enter_context(nc.sbuf_tensor("delta_sb", [128, NCG * TILE], F32))
                mu_sb = _es.enter_context(nc.sbuf_tensor("mu_sb", [128, TILE], F32))
                msq_sb = _es.enter_context(nc.sbuf_tensor("msq_sb", [128, TILE], F32))
                var_sb = _es.enter_context(nc.sbuf_tensor("var_sb", [128, TILE], F32))
                c_zL = _es.enter_context(nc.sbuf_tensor("c_zL", [128, L], F32))
                rstd_sb = _es.enter_context(nc.sbuf_tensor("rstd_sb", [128, TILE], F32))
                shift_sb = _es.enter_context(nc.sbuf_tensor("shift_sb", [128, TILE], F32))
                mux_sb = _es.enter_context(nc.sbuf_tensor("mux_sb", [128, NCG * TILE * 2], BF16))
                r_sb = _es.enter_context(nc.sbuf_tensor("r_sb", [128, NSG * TILE], F32))
                rbf_sb = _es.enter_context(nc.sbuf_tensor("rbf_sb", [128, NCG * TILE], BF16))
                k_sb = _es.enter_context(nc.sbuf_tensor("k_sb", [128, NSG * TILE], F32))
                vv_sb = _es.enter_context(nc.sbuf_tensor("vv_sb", [128, NSG * TILE], F32))
                t1_sb = _es.enter_context(nc.sbuf_tensor("t1_sb", [64, TILE], BF16))
                d_sb = _es.enter_context(nc.sbuf_tensor("d_sb", [128, NSG * TILE], F32))
                di_sb = _es.enter_context(nc.sbuf_tensor("di_sb", [128, NSG * TILE], F32))
                a_sb = _es.enter_context(nc.sbuf_tensor("a_sb", [128, NSG * TILE], F32))
                gp_sb = _es.enter_context(nc.sbuf_tensor("gp_sb", [128, TILE], BF16))
                g_sb = _es.enter_context(nc.sbuf_tensor("g_sb", [128, NSG * TILE], F32))
                ktmp_sb = x_sb[:, 0:NSG * TILE]
                bt_sb = x_sb[:, NSG * TILE:2 * NSG * TILE]
                kk_sb = delta_sb[:, 0:NSG * TILE]
                kfin_sb = _es.enter_context(nc.sbuf_tensor("kfin_sb", [128, NSG * TILE], F32))
                cbuf_sb = _es.enter_context(nc.sbuf_tensor("cbuf_sb", [128, NSG * KPT * LP], F32))
                cibuf_sb = _es.enter_context(nc.sbuf_tensor("cibuf_sb", [128, NSG * KPT * LP], F32))
                ecw_sb = delta_sb[:, NSG * TILE:2 * NSG * TILE]
                pA = _es.enter_context(nc.sbuf_tensor("pA", [128, NSG * KPT * 64], BF16))
                pB = _es.enter_context(nc.sbuf_tensor("pB", [128, NSG * KPT * 64], BF16))
                pK = _es.enter_context(nc.sbuf_tensor("pK", [128, NSG * KPT * 64], BF16))
                pR = _es.enter_context(nc.sbuf_tensor("pR", [128, NSG * KPT * 64], BF16))
                pBB = _es.enter_context(nc.sbuf_tensor("pBB", [128, NSG * KPT * 64], BF16))
                pKB = _es.enter_context(nc.sbuf_tensor("pKB", [128, NSG * KPT * 64], BF16))
                pV = _es.enter_context(nc.sbuf_tensor("pV", [128, NSG * KPT * 64], BF16))
                tA = _es.enter_context(nc.sbuf_tensor("tA", [128, NSG * KPT * 64], BF16))
                tV = _es.enter_context(nc.sbuf_tensor("tV", [128, NSG * KPT * 64], BF16))
                tBB = _es.enter_context(nc.sbuf_tensor("tBB", [128, NSG * KPT * 64], BF16))
                tKB = _es.enter_context(nc.sbuf_tensor("tKB", [128, NSG * KPT * 64], BF16))
                mT = _es.enter_context(nc.sbuf_tensor("mT", [128, 256], BF16))
                mTT = _es.enter_context(nc.sbuf_tensor("mTT", [128, 256], BF16))
                mT2 = _es.enter_context(nc.sbuf_tensor("mT2", [128, 256], BF16))
                mTT2 = _es.enter_context(nc.sbuf_tensor("mTT2", [128, 256], BF16))
                mT4 = _es.enter_context(nc.sbuf_tensor("mT4", [128, 256], BF16))
                mTT4 = _es.enter_context(nc.sbuf_tensor("mTT4", [128, 256], BF16))
                mT8 = _es.enter_context(nc.sbuf_tensor("mT8", [128, 256], BF16))
                mTT8 = _es.enter_context(nc.sbuf_tensor("mTT8", [128, 256], BF16))
                mT16 = _es.enter_context(nc.sbuf_tensor("mT16", [128, 256], BF16))
                mYT = _es.enter_context(nc.sbuf_tensor("mYT", [128, 256], BF16))
                mPT = _es.enter_context(nc.sbuf_tensor("mPT", [128, 256], BF16))
                mQT = _es.enter_context(nc.sbuf_tensor("mQT", [128, 256], BF16))
                mWa = _es.enter_context(nc.sbuf_tensor("mWa", [128, 256], BF16))
                mWb = _es.enter_context(nc.sbuf_tensor("mWb", [128, 256], BF16))
                mWT = _es.enter_context(nc.sbuf_tensor("mWT", [128, 256], BF16))
                mDT = _es.enter_context(nc.sbuf_tensor("mDT", [128, NSG * 64], BF16))
                mRHS2 = _es.enter_context(nc.sbuf_tensor("mRHS2", [128, NSG * 128], BF16))
                mU = _es.enter_context(nc.sbuf_tensor("mU", [128, NSG * 128], BF16))
                o_sb = _es.enter_context(nc.sbuf_tensor("o_sb", [128, NSG * TILE], F32))
                osq_sb = _es.enter_context(nc.sbuf_tensor("osq_sb", [128, NSG * TILE], F32))
                gmu_sb = _es.enter_context(nc.sbuf_tensor("gmu_sb", [128, NSG * TILE], F32))
                gs1_sb = _es.enter_context(nc.sbuf_tensor("gs1_sb", [128, NSG * TILE], F32))
                og_sb = _es.enter_context(nc.sbuf_tensor("og_sb", [128, NSG * TILE], BF16))
                s_in.wait(vec)
                vec.memset(c_zL[:, :], 0.0)
                vec.memset(st_sb[:, :], 0.0)
                vec.memset(stbf_sb[:, :], 0.0)
                vec.memset(cbuf_sb[:].rearrange("p (c u) -> p c u", u=LP)[:, :, 0:1], 1.0)
                vec.memset(cibuf_sb[:].rearrange("p (c u) -> p c u", u=LP)[:, :, 0:1], 1.0)
                vec.memset(xn_sb[:].rearrange("p (g n) -> p g n", n=TILE + 1)[:, :, 0:1], 0.0)
                for buf in (pA, pB, pK, pR, pBB, pKB, pV, tA, tV, tBB, tKB):
                    vec.memset(buf[:, :], 0.0)
                s_dv.inc(vec.memset(og_sb[:, :], 0.0))

                def xnv(g):
                    return xn_sb[:, g * (TILE + 1):(g + 1) * (TILE + 1)]

                flip = [0]

                def evac_plain(dst, src):
                    flip[0] ^= 1
                    if flip[0]:
                        s_ac.inc(act.copy(dst, src))
                    else:
                        s_dv.inc(vec.tensor_copy(dst, src))

                def pblk(buf, g, k):
                    return buf[:, (g * KPT + k) * 64:(g * KPT + k) * 64 + 64]

                def tblk(buf, g, k):
                    r0 = (k % 2) * 64
                    return buf[r0:r0 + 64, (g * KPT // 2 + k // 2) * 128:
                               (g * KPT // 2 + k // 2) * 128 + 128]

                def mblk(buf, g):
                    return buf[0:64, g * 64:g * 64 + 64]

                def mblk2(buf, g):
                    return buf[0:64, g * 128:g * 128 + 128]

                def cview(buf, g, half, incl):
                    r0 = 64 * half
                    v = buf[r0:r0 + 64, g * KPT * LP:(g + 1) * KPT * LP]
                    v = v.rearrange("p (k u) -> p k u", u=LP)
                    return v[:, :, 1:33] if incl else v[:, :, 0:32]

                def flatv(buf, g, half):
                    r0 = 64 * half
                    return buf[r0:r0 + 64, g * TILE:(g + 1) * TILE].rearrange(
                        "p (k u) -> p k u", u=L)

                def padv(buf, g, half):
                    r0 = 64 * half
                    v = buf[r0:r0 + 64, g * KPT * 64:(g + 1) * KPT * 64]
                    v = v.rearrange("p (k u) -> p k u", u=64)
                    return v[:, :, 32 * half:32 * half + 32]

                for ti in range(BUILD_TILES):
                    t0 = ti * TILE
                    bg = ti // 4
                    if STAGE_LIMIT < 1.005:
                        continue
                    with contextlib.ExitStack() as _es:
                        p_a = _es.enter_context(nc.psum_tensor(f"p_a_{ti}", [128, TILE], F32))
                        p_b = _es.enter_context(nc.psum_tensor(f"p_b_{ti}", [128, TILE], F32))
                        p_c = _es.enter_context(nc.psum_tensor(f"p_c_{ti}", [128, TILE], F32))
                        p_sm = _es.enter_context(nc.psum_tensor(f"p_sm_{ti}", [128, 256], F32))
                        p_sm2 = _es.enter_context(nc.psum_tensor(f"p_sm2_{ti}", [128, 256], F32))
                        p_tr = _es.enter_context(nc.psum_tensor(f"p_tr_{ti}", [128, 128], BF16))
                        p_st = _es.enter_context(nc.psum_tensor(f"p_st_{ti}", [128, NSG * 128], F32))
                        p_su = _es.enter_context(nc.psum_tensor(f"p_su_{ti}", [128, NSG * 128], F32))
                        if STAGE_LIMIT < 1.05:
                            continue
                        s_xt.dinc(syn.dma_start(
                            out=x_sb[:].rearrange("p (g n) -> p g n", n=TILE),
                            in_=x_cm[:, t0:t0 + TILE].rearrange("(g p) n -> p g n", p=128)))
                        if STAGE_LIMIT < 1.08:
                            continue
                        wait_all((s_in, s_wt, s_xt, s_dv, s_ac), (tsr, vec, act))

                        if STAGE_LIMIT < 1.2:
                            continue
                        s_ac.inc(act.activation(delta_sb[:, :], x_sb[:, :], AF.Square))
                        s_ac.wait(tsr)
                        for g in range(NCG):
                            tsr.matmul(p_a[:, :], c_ones[:, :],
                                       x_sb[:, g * TILE:(g + 1) * TILE],
                                       start=(g == 0), stop=(g == NCG - 1))
                        for g in range(NCG):
                            ins = tsr.matmul(p_b[:, :], c_ones[:, :],
                                             delta_sb[:, g * TILE:(g + 1) * TILE],
                                             start=(g == 0), stop=(g == NCG - 1))
                        s_pe.inc(ins)
                        if STAGE_LIMIT < 1.3:
                            continue
                        s_pe.wait(vec)
                        vec.tensor_scalar(mu_sb[:, :], p_a[:, :], 1.0 / C, None, ALU.mult)
                        vec.tensor_mul(msq_sb[:, :], mu_sb[:, :], mu_sb[:, :])
                        s_dv.inc(vec.scalar_tensor_tensor(
                            var_sb[:, :], p_b[:, :], 1.0 / C, msq_sb[:, :],
                            ALU.mult, ALU.subtract))
                        if STAGE_LIMIT < 1.4:
                            continue
                        s_dv.wait(act)
                        act.activation(rstd_sb[:, :], var_sb[:, :], AF.Ln, bias=c_eps[:, :])
                        s_ac.inc(act.activation(rstd_sb[:, :], rstd_sb[:, :], AF.Exp, scale=-0.5))
                        if STAGE_LIMIT < 1.5:
                            continue
                        s_ac.wait(vec)
                        s_dv.inc(vec.scalar_tensor_tensor(
                            shift_sb[:, :], mu_sb[:, :], -1.0, rstd_sb[:, :],
                            ALU.mult, ALU.mult))
                        if ti > 0:
                            vec.tensor_copy(
                                xn_sb[:].rearrange("p (g n) -> p g n", n=TILE + 1)[:, :, 0:1],
                                xn_sb[:].rearrange("p (g n) -> p g n", n=TILE + 1)[:, :, TILE:TILE + 1])
                        for g in range(NCG):
                            vec.tensor_tensor(delta_sb[:, g * TILE:(g + 1) * TILE],
                                              x_sb[:, g * TILE:(g + 1) * TILE],
                                              rstd_sb[:, :], ALU.mult)
                        for g in range(NCG):
                            ins = vec.tensor_tensor(xnv(g)[:, 1:TILE + 1],
                                                    delta_sb[:, g * TILE:(g + 1) * TILE],
                                                    shift_sb[:, :], ALU.add)
                        s_dv.inc(ins)
                        if ti == BUILD_TILES - 1:
                            tapdma("xn", xn_sb[:, :], 128, NCG * (TILE + 1))
                        for g in range(NCG):
                            ins = vec.tensor_tensor(delta_sb[:, g * TILE:(g + 1) * TILE],
                                                    xnv(g)[:, 0:TILE],
                                                    xnv(g)[:, 1:TILE + 1], ALU.subtract)
                        s_dv.inc(ins)

                        if STAGE_LIMIT < 1.6:
                            continue

                        def mux(slot, mi):
                            ins = None
                            for g in range(NCG):
                                ins = vec.scalar_tensor_tensor(
                                    mux_sb[:, (slot * NCG + g) * TILE:(slot * NCG + g + 1) * TILE],
                                    delta_sb[:, g * TILE:(g + 1) * TILE],
                                    v_xmix[:, mi * NCG + g:mi * NCG + g + 1],
                                    xnv(g)[:, 1:TILE + 1], ALU.mult, ALU.add)
                            s_dv.inc(ins)

                        def mxs(slot, g):
                            return mux_sb[:, (slot * NCG + g) * TILE:(slot * NCG + g + 1) * TILE]

                        if STAGE_LIMIT >= 2:
                            mux(0, 0)
                            mux(1, 1)
                            s_dv.wait(tsr)
                            for mc in range(NCG):
                                pp = p_a if mc % 2 == 0 else p_b
                                if mc >= 2:
                                    (s_dv if mc % 2 == 0 else s_ac).wait(tsr)
                                for kc in range(NCG):
                                    ins = tsr.matmul(pp[:, :],
                                                     wr_sb[:, kc * C + mc * 128:kc * C + mc * 128 + 128],
                                                     mxs(0, kc), start=(kc == 0), stop=(kc == NCG - 1))
                                s_pe.inc(ins)
                                s_pe.wait(vec if mc % 2 == 0 else act)
                                if mc < NSG:
                                    if mc % 2 == 0:
                                        s_dv.inc(vec.tensor_copy(r_sb[:, mc * TILE:(mc + 1) * TILE], pp[:, :]))
                                    else:
                                        s_ac.inc(act.copy(r_sb[:, mc * TILE:(mc + 1) * TILE], pp[:, :]))
                                else:
                                    if mc % 2 == 0:
                                        s_dv.inc(vec.tensor_copy(rbf_sb[:, mc * TILE:(mc + 1) * TILE], pp[:, :]))
                                    else:
                                        s_ac.inc(act.copy(rbf_sb[:, mc * TILE:(mc + 1) * TILE], pp[:, :]))
                            s_dv.wait(act)
                            s_ac.inc(act.copy(rbf_sb[:, 0:NSG * TILE], r_sb[:, :]))
                            if ti == BUILD_TILES - 1:
                                tapdma("r", r_sb[:, :], 128, NSG * TILE)
                            for kc in range(NCG):
                                ins = tsr.matmul(p_sm[0:64, 0:TILE],
                                                 wla_sb[:, kc * 64:(kc + 1) * 64],
                                                 mxs(1, kc), start=(kc == 0), stop=(kc == NCG - 1))
                            s_pe.inc(ins)
                            s_pe.wait(act)
                            s_ac.inc(act.activation(t1_sb[:, :], p_sm[0:64, 0:TILE], AF.Tanh))
                            s_pe.wait(vec)
                            mux(0, 2)
                            mux(1, 3)
                            s_dv.wait(tsr)
                            for mc in range(NSG):
                                pp = p_a if mc % 2 == 0 else p_b
                                if mc >= 2:
                                    (s_dv if mc % 2 == 0 else s_ac).wait(tsr)
                                for kc in range(NCG):
                                    ins = tsr.matmul(pp[:, :],
                                                     wk_sb[:, kc * CPC + mc * 128:kc * CPC + mc * 128 + 128],
                                                     mxs(0, kc), start=(kc == 0), stop=(kc == NCG - 1))
                                s_pe.inc(ins)
                                s_pe.wait(vec if mc % 2 == 0 else act)
                                if mc % 2 == 0:
                                    s_dv.inc(vec.tensor_copy(k_sb[:, mc * TILE:(mc + 1) * TILE], pp[:, :]))
                                else:
                                    s_ac.inc(act.copy(k_sb[:, mc * TILE:(mc + 1) * TILE], pp[:, :]))
                            for mc in range(NSG):
                                pp = p_a if mc % 2 == 0 else p_b
                                (s_dv if mc % 2 == 0 else s_ac).wait(tsr)
                                for kc in range(NCG):
                                    ins = tsr.matmul(pp[:, :],
                                                     wv_sb[:, kc * CPC + mc * 128:kc * CPC + mc * 128 + 128],
                                                     mxs(1, kc), start=(kc == 0), stop=(kc == NCG - 1))
                                s_pe.inc(ins)
                                s_pe.wait(vec if mc % 2 == 0 else act)
                                if mc % 2 == 0:
                                    s_dv.inc(vec.tensor_copy(vv_sb[:, mc * TILE:(mc + 1) * TILE], pp[:, :]))
                                else:
                                    s_ac.inc(act.copy(vv_sb[:, mc * TILE:(mc + 1) * TILE], pp[:, :]))
                            s_ac.wait(tsr)
                            for g in range(NSG):
                                s_ac.wait(tsr)
                                ins = tsr.matmul(p_c[:, :], wlb_sb[:, g * 128:(g + 1) * 128],
                                                 t1_sb[:, :], start=True, stop=True)
                                s_pe.inc(ins)
                                s_pe.wait(act)
                                act.activation(d_sb[:, g * TILE:(g + 1) * TILE], p_c[:, :],
                                               AF.Tanh, bias=v_wlbh[:, g:g + 1], scale=0.5)
                                ins = act.activation(di_sb[:, g * TILE:(g + 1) * TILE], p_c[:, :],
                                                     AF.Exp, bias=v_wlbn[:, g:g + 1], scale=-1.0)
                                s_ac.inc(ins)
                            s_ac.wait(vec)
                            vec.tensor_scalar(d_sb[:, :], d_sb[:, :],
                                              0.5 * E_MHALF, 0.5 * E_MHALF, ALU.mult, ALU.add)
                            s_dv.inc(vec.tensor_scalar(di_sb[:, :], di_sb[:, :],
                                                       E_HALF, E_HALF, ALU.mult, ALU.add))
                            s_pe.wait(vec)
                            mux(0, 4)
                            s_dv.wait(tsr)
                            for kc in range(NCG):
                                ins = tsr.matmul(p_sm2[0:64, 0:TILE],
                                                 ala_sb[:, kc * 64:(kc + 1) * 64],
                                                 mxs(0, kc), start=(kc == 0), stop=(kc == NCG - 1))
                            s_pe.inc(ins)
                            s_pe.wait(act)
                            s_ac.inc(act.copy(t1_sb[:, :], p_sm2[0:64, 0:TILE]))
                            s_ac.wait(tsr)
                            for g in range(NSG):
                                s_ac.wait(tsr)
                                ins = tsr.matmul(p_c[:, :], alb_sb[:, g * 128:(g + 1) * 128],
                                                 t1_sb[:, :], start=True, stop=True)
                                s_pe.inc(ins)
                                s_pe.wait(act)
                                ins = act.activation(a_sb[:, g * TILE:(g + 1) * TILE], p_c[:, :],
                                                     AF.Tanh, bias=v_albh[:, g:g + 1], scale=0.5)
                                s_ac.inc(ins)
                            s_ac.wait(vec)
                            s_dv.inc(vec.tensor_scalar(a_sb[:, :], a_sb[:, :], 0.5, 0.5,
                                                       ALU.mult, ALU.add))
                            s_ac.wait(tsr)
                            for kc in range(NCG):
                                ins = tsr.matmul(p_c[:, :], gla_sb[:, kc * 128:(kc + 1) * 128],
                                                 rbf_sb[:, kc * TILE:(kc + 1) * TILE],
                                                 start=(kc == 0), stop=(kc == NCG - 1))
                            s_pe.inc(ins)
                            s_pe.wait(act)
                            s_ac.inc(act.activation(gp_sb[:, :], p_c[:, :], AF.Tanh, scale=0.5))
                            s_ac.wait(vec)
                            s_dv.inc(vec.tensor_scalar(gp_sb[:, :], gp_sb[:, :], 0.5, 0.5,
                                                       ALU.mult, ALU.add))
                            s_dv.wait(tsr)
                            for g in range(NSG):
                                s_ac.wait(tsr)
                                ins = tsr.matmul(p_c[:, :], glb_sb[:, g * 128:(g + 1) * 128],
                                                 gp_sb[:, :], start=True, stop=True)
                                s_pe.inc(ins)
                                s_pe.wait(act)
                                ins = act.activation(g_sb[:, g * TILE:(g + 1) * TILE], p_c[:, :],
                                                     AF.Identity, bias=v_gb[:, g:g + 1])
                                s_ac.inc(ins)
                            if ti == BUILD_TILES - 1:
                                tapdma("g", g_sb[:, :], 128, NSG * TILE)
                                tapdma("d", d_sb[:, :], 128, NSG * TILE)
                                tapdma("a", a_sb[:, :], 128, NSG * TILE)
                                tapdma("v", vv_sb[:, :], 128, NSG * TILE)

                        if STAGE_LIMIT >= 3:
                            s_dv.wait(vec)
                            s_ac.wait(vec)
                            for g in range(NSG):
                                ins = vec.tensor_scalar(ktmp_sb[:, g * TILE:(g + 1) * TILE],
                                                        k_sb[:, g * TILE:(g + 1) * TILE],
                                                        v_kk[:, g:g + 1], None, ALU.mult)
                            s_dv.inc(ins)
                            s_dv.wait(act)
                            s_pe.wait(act)
                            s_ac.inc(act.activation(osq_sb[:, :], ktmp_sb[:, :], AF.Square))
                            s_ac.wait(tsr)
                            for g in range(NSG):
                                if g >= 2:
                                    s_ac.wait(tsr)
                                ins = tsr.matmul(p_a[:, :] if g % 2 == 0 else p_b[:, :],
                                                 c_ho[:, :], osq_sb[:, g * TILE:(g + 1) * TILE],
                                                 start=True, stop=True)
                                s_pe.inc(ins)
                                s_pe.wait(act)
                                ins = act.activation(kk_sb[:, g * TILE:(g + 1) * TILE],
                                                     p_a[:, :] if g % 2 == 0 else p_b[:, :],
                                                     AF.Ln)
                                s_ac.inc(ins)
                            s_ac.wait(act)
                            s_ac.inc(act.activation(kk_sb[:, :], kk_sb[:, :], AF.Exp, scale=-0.5))
                            s_ac.wait(vec)
                            vec.tensor_scalar_min(kk_sb[:, :], kk_sb[:, :], 1e12)
                            s_dv.inc(vec.tensor_mul(kk_sb[:, :], ktmp_sb[:, :], kk_sb[:, :]))
                            for g in range(NSG):
                                vec.tensor_scalar(kfin_sb[:, g * TILE:(g + 1) * TILE],
                                                  a_sb[:, g * TILE:(g + 1) * TILE],
                                                  v_ka[:, g:g + 1], v_1mka[:, g:g + 1],
                                                  ALU.mult, ALU.add)
                            vec.tensor_mul(kfin_sb[:, :], k_sb[:, :], kfin_sb[:, :])
                            s_dv.inc(vec.tensor_mul(bt_sb[:, :], kk_sb[:, :], a_sb[:, :]))
                            if ti == BUILD_TILES - 1:
                                tapdma("kk", kk_sb[:, :], 128, NSG * TILE)
                                tapdma("kfin", kfin_sb[:, :], 128, NSG * TILE)

                        if STAGE_LIMIT >= 4:
                            zz = c_zL
                            for g in range(NSG):
                                for kq in range(KPT):
                                    base = (g * KPT + kq) * LP
                                    vec.tensor_tensor_scan(
                                        cbuf_sb[:, base + 1:base + 33],
                                        d_sb[:, g * TILE + kq * L:g * TILE + kq * L + L],
                                        zz[:, 0:L], 1.0, ALU.mult, ALU.add)
                                    vec.tensor_tensor_scan(
                                        cibuf_sb[:, base + 1:base + 33],
                                        di_sb[:, g * TILE + kq * L:g * TILE + kq * L + L],
                                        zz[:, 0:L], 1.0, ALU.mult, ALU.add)
                            ins = None
                            for g in range(NSG):
                                for kq in range(KPT):
                                    base = (g * KPT + kq) * LP
                                    ins = vec.tensor_scalar(
                                        ecw_sb[:, g * TILE + kq * L:g * TILE + kq * L + L],
                                        cibuf_sb[:, base + 1:base + 33],
                                        cbuf_sb[:, base + 32:base + 33], None, ALU.mult)
                            s_dv.inc(ins)
                            if ti == BUILD_TILES - 1:
                                tapdma("cbuf", cbuf_sb[:, :], 128, NSG * KPT * LP)
                                tapdma("cibuf", cibuf_sb[:, :], 128, NSG * KPT * LP)
                                tapdma("ecw", ecw_sb[:, :], 128, NSG * TILE)
                                tapdma("dd", d_sb[:, :], 128, NSG * TILE)
                                tapdma("di", di_sb[:, :], 128, NSG * TILE)

                        if STAGE_LIMIT >= 5:
                            ins = None
                            for g in range(NSG):
                                for hf in range(2):
                                    ins = vec.scalar_tensor_tensor(
                                        padv(pA, g, hf), flatv(kk_sb, g, hf), -1.0,
                                        cview(cbuf_sb, g, hf, False), ALU.mult, ALU.mult)
                                    ins = vec.tensor_tensor(
                                        padv(pB, g, hf), flatv(bt_sb, g, hf),
                                        cview(cibuf_sb, g, hf, True), ALU.mult)
                                    ins = vec.tensor_tensor(
                                        padv(pK, g, hf), flatv(kfin_sb, g, hf),
                                        cview(cibuf_sb, g, hf, True), ALU.mult)
                                    ins = vec.tensor_tensor(
                                        padv(pR, g, hf), flatv(r_sb, g, hf),
                                        cview(cbuf_sb, g, hf, True), ALU.mult)
                                    ins = vec.tensor_tensor(
                                        padv(pBB, g, hf), flatv(bt_sb, g, hf),
                                        flatv(ecw_sb, g, hf), ALU.mult)
                                    ins = vec.tensor_tensor(
                                        padv(pKB, g, hf), flatv(kfin_sb, g, hf),
                                        flatv(ecw_sb, g, hf), ALU.mult)
                                    ins = vec.tensor_copy(padv(pV, g, hf), flatv(vv_sb, g, hf))
                            s_dv.inc(ins)
                            if ti == BUILD_TILES - 1:
                                tapdma("pA", pA[:, :], 128, NSG * KPT * 64, BF16)
                                tapdma("pV", pV[:, :], 128, NSG * KPT * 64, BF16)

                        if STAGE_LIMIT >= 6:
                            s_dv.wait(tsr)
                            for buf, tbuf in ((pA, tA), (pV, tV), (pBB, tBB), (pKB, tKB)):
                                for g in range(NSG):
                                    for k2 in range(KPT // 2):
                                        cofs = g * KPT * 64 + k2 * 128
                                        s_dv.wait(tsr)
                                        ins = tsr.transpose(p_tr[:, :],
                                                            buf[:, cofs:cofs + 128], c_id[:, :])
                                        s_pe.inc(ins)
                                        s_pe.wait(vec)
                                        s_dv.inc(vec.tensor_copy(tbuf[:, cofs:cofs + 128], p_tr[:, :]))
                            if ti == BUILD_TILES - 1:
                                tapdma("pB", pB[:, :], 128, NSG * KPT * 64, BF16)
                                tapdma("pK", pK[:, :], 128, NSG * KPT * 64, BF16)
                                tapdma("pR", pR[:, :], 128, NSG * KPT * 64, BF16)
                                tapdma("pBB", pBB[:, :], 128, NSG * KPT * 64, BF16)
                                tapdma("pKB", pKB[:, :], 128, NSG * KPT * 64, BF16)
                                tapdma("tA", tA[:, :], 128, NSG * KPT * 64, BF16)
                                tapdma("tV", tV[:, :], 128, NSG * KPT * 64, BF16)
                                tapdma("tBB", tBB[:, :], 128, NSG * KPT * 64, BF16)
                                tapdma("tKB", tKB[:, :], 128, NSG * KPT * 64, BF16)

                        if STAGE_LIMIT >= 7:
                            for kq in range(KPT):
                                pr = 64 * (kq % 2)

                                def mb(buf, g):
                                    return buf[pr:pr + 64, g * 64:g * 64 + 64]

                                def mb2(buf, g):
                                    return buf[pr:pr + 64, g * 128:g * 128 + 128]

                                ps1 = p_sm[pr:pr + 64, :]
                                ps2 = p_sm2[pr:pr + 64, :]
                                msl = c_msl[pr:pr + 64, :]
                                msu = c_msu[pr:pr + 64, :]
                                mui = c_mui[pr:pr + 64, :]
                                i64 = c_i64[pr:pr + 64, :]
                                s_dv.wait(tsr)
                                for g in range(NSG):
                                    tsr.matmul(ps1[:, g * 64:g * 64 + 64],
                                               pblk(pA, g, kq), pblk(pB, g, kq),
                                               start=True, stop=True)
                                for g in range(NSG):
                                    ins = tsr.matmul(ps2[:, g * 64:g * 64 + 64],
                                                     pblk(pB, g, kq), pblk(pA, g, kq),
                                                     start=True, stop=True)
                                s_pe.inc(ins)
                                s_pe.wait(vec)
                                vec.tensor_tensor(mT[pr:pr + 64, :], ps1, msl, ALU.mult)
                                vec.tensor_tensor(mTT[pr:pr + 64, :], ps2, msu, ALU.mult)
                                s_dv.inc(vec.tensor_tensor(mWa[pr:pr + 64, :], mTT[pr:pr + 64, :], i64, ALU.add))
                                if ti == BUILD_TILES - 1 and kq == 0:
                                    tapdma("mT", mT[:, :], 128, 256, BF16)
                                    tapdma("mTT", mTT[:, :], 128, 256, BF16)
                                s_dv.wait(tsr)
                                for g in range(NSG):
                                    tsr.matmul(ps1[:, g * 64:g * 64 + 64],
                                               pblk(pK, g, kq), pblk(pA, g, kq),
                                               start=True, stop=True)
                                for g in range(NSG):
                                    ins = tsr.matmul(ps2[:, g * 64:g * 64 + 64],
                                                     pblk(pB, g, kq), pblk(pR, g, kq),
                                                     start=True, stop=True)
                                s_pe.inc(ins)
                                s_pe.wait(vec)
                                vec.tensor_tensor(mYT[pr:pr + 64, :], ps1, msu, ALU.mult)
                                s_dv.inc(vec.tensor_tensor(mPT[pr:pr + 64, :], ps2, mui, ALU.mult))
                                s_dv.wait(tsr)
                                for g in range(NSG):
                                    ins = tsr.matmul(ps1[:, g * 64:g * 64 + 64],
                                                     pblk(pK, g, kq), pblk(pR, g, kq),
                                                     start=True, stop=True)
                                s_pe.inc(ins)
                                s_pe.wait(vec)
                                s_dv.inc(vec.tensor_tensor(mQT[pr:pr + 64, :], ps1, mui, ALU.mult))
                                s_dv.wait(tsr)

                                def sqwave(lhs_a, rhs_a, out_a, lhs_b, rhs_b, out_b):
                                    for g in range(NSG):
                                        tsr.matmul(ps1[:, g * 64:g * 64 + 64],
                                                   mb(lhs_a, g), mb(rhs_a, g),
                                                   start=True, stop=True)
                                    ins = None
                                    for g in range(NSG):
                                        ins = tsr.matmul(ps2[:, g * 64:g * 64 + 64],
                                                         mb(lhs_b, g), mb(rhs_b, g),
                                                         start=True, stop=True)
                                    s_pe.inc(ins)
                                    s_pe.wait(vec)
                                    vec.tensor_copy(out_a[pr:pr + 64, :], ps1)
                                    s_dv.inc(vec.tensor_copy(out_b[pr:pr + 64, :], ps2))
                                    s_dv.wait(tsr)

                                sqwave(mTT, mT, mT2, mT, mTT, mTT2)
                                sqwave(mTT2, mT2, mT4, mT2, mTT2, mTT4)
                                sqwave(mTT4, mT4, mT8, mT4, mTT4, mTT8)
                                for g in range(NSG):
                                    tsr.matmul(ps1[:, g * 64:g * 64 + 64],
                                               mb(mTT8, g), mb(mT8, g), start=True, stop=True)
                                for g in range(NSG):
                                    ins = tsr.matmul(ps2[:, g * 64:g * 64 + 64],
                                                     mb(mT2, g), mb(mWa, g), start=True, stop=True)
                                s_pe.inc(ins)
                                s_pe.wait(vec)
                                vec.tensor_copy(mT16[pr:pr + 64, :], ps1)
                                s_dv.inc(vec.tensor_tensor(mWb[pr:pr + 64, :], ps2, mWa[pr:pr + 64, :], ALU.add))
                                s_dv.wait(tsr)
                                for g in range(NSG):
                                    ins = tsr.matmul(ps1[:, g * 64:g * 64 + 64],
                                                     mb(mT4, g), mb(mWb, g), start=True, stop=True)
                                s_pe.inc(ins)
                                s_pe.wait(vec)
                                s_dv.inc(vec.tensor_tensor(mWa[pr:pr + 64, :], ps1, mWb[pr:pr + 64, :], ALU.add))
                                s_dv.wait(tsr)
                                for g in range(NSG):
                                    ins = tsr.matmul(ps1[:, g * 64:g * 64 + 64],
                                                     mb(mT8, g), mb(mWa, g), start=True, stop=True)
                                s_pe.inc(ins)
                                s_pe.wait(vec)
                                s_dv.inc(vec.tensor_tensor(mWb[pr:pr + 64, :], ps1, mWa[pr:pr + 64, :], ALU.add))
                                s_dv.wait(tsr)
                                for g in range(NSG):
                                    ins = tsr.matmul(ps1[:, g * 64:g * 64 + 64],
                                                     mb(mT16, g), mb(mWb, g), start=True, stop=True)
                                s_pe.inc(ins)
                                s_pe.wait(vec)
                                s_dv.inc(vec.tensor_tensor(mWT[pr:pr + 64, :], ps1, mWb[pr:pr + 64, :], ALU.add))
                                if ti == BUILD_TILES - 1 and kq == 0:
                                    tapdma("mWT", mWT[:, :], 128, 256, BF16)
                                s_dv.wait(tsr)
                                for g in range(NSG):
                                    tsr.matmul(p_su[:, g * 64:g * 64 + 64],
                                               tblk(tA, g, kq), mb(mWT, g), start=True, stop=True)
                                for g in range(NSG):
                                    ins = tsr.matmul(p_st[pr:pr + 64, g * 128:g * 128 + 128],
                                                     mb(mYT, g), tblk(tV, g, kq),
                                                     start=True, stop=True)
                                s_pe.inc(ins)
                                s_pe.wait(vec)
                                vec.tensor_copy(mDT[:, :], p_su[:, 0:NSG * 64])
                                s_dv.inc(vec.tensor_copy(mRHS2[pr:pr + 64, :], p_st[pr:pr + 64, :]))
                                if ti == BUILD_TILES - 1 and kq == 0:
                                    tapdma("mDT", mDT[:, :], 128, NSG * 64, BF16)
                                    tapdma("mRHS2", mRHS2[:, :], 128, NSG * 128, BF16)
                                s_dv.wait(tsr)
                                for g in range(NSG):
                                    tsr.matmul(p_st[pr:pr + 64, g * 128:g * 128 + 128],
                                               mDT[:, g * 64:g * 64 + 64],
                                               stbf_sb[:, g * 128:g * 128 + 128],
                                               start=True, stop=False)
                                    ins = tsr.matmul(p_st[pr:pr + 64, g * 128:g * 128 + 128],
                                                     mb(mWT, g), mb2(mRHS2, g),
                                                     start=False, stop=True)
                                s_pe.inc(ins)
                                s_pe.wait(vec)
                                s_dv.inc(vec.tensor_copy(mU[pr:pr + 64, :], p_st[pr:pr + 64, :]))
                                if ti == BUILD_TILES - 1 and kq == 0:
                                    tapdma("mU", mU[:, :], 128, NSG * 128, BF16)
                                s_dv.wait(tsr)
                                for g in range(NSG):
                                    tsr.matmul(p_c[:, g * 64:g * 64 + 64],
                                               stbf_sb[:, g * 128:g * 128 + 128],
                                               pblk(pR, g, kq), start=True, stop=False)
                                    tsr.matmul(p_c[:, g * 64:g * 64 + 64],
                                               mb2(mU, g), mb(mPT, g), start=False, stop=False)
                                    ins = tsr.matmul(p_c[:, g * 64:g * 64 + 64],
                                                     tblk(tV, g, kq), mb(mQT, g),
                                                     start=False, stop=True)
                                s_pe.inc(ins)
                                for g in range(NSG):
                                    tsr.matmul(p_su[:, g * 128:g * 128 + 128],
                                               tblk(tBB, g, kq), mb2(mU, g),
                                               start=True, stop=False)
                                    ins = tsr.matmul(p_su[:, g * 128:g * 128 + 128],
                                                     tblk(tKB, g, kq), tblk(tV, g, kq),
                                                     start=False, stop=True)
                                s_pe.inc(ins)
                                s_pe.wait(vec)
                                for hf in range(2):
                                    r0 = 64 * hf
                                    ins = vec.tensor_copy(
                                        o_sb[r0:r0 + 64, 0:NSG * TILE].rearrange(
                                            "p (g t) -> p g t", t=TILE)[:, :, kq * L:kq * L + L],
                                        p_c[r0:r0 + 64, :].rearrange(
                                            "p (g u) -> p g u", u=64)[:, :, 32 * hf:32 * hf + 32])
                                s_dv.inc(ins)
                                for g in range(NSG):
                                    base = (g * KPT + kq) * LP
                                    vec.scalar_tensor_tensor(
                                        st_sb[:, g * 128:g * 128 + 128],
                                        st_sb[:, g * 128:g * 128 + 128],
                                        cbuf_sb[:, base + 32:base + 33],
                                        p_su[:, g * 128:g * 128 + 128], ALU.mult, ALU.add)
                                ins = vec.tensor_copy(stbf_sb[:, :], st_sb[:, :])
                                s_dv.inc(ins)
                            if ti == BUILD_TILES - 1:
                                tapdma("o", o_sb[:, :], 128, NSG * TILE)
                                tapdma("st", st_sb[:, :], 128, NSG * 128)
                        if STAGE_LIMIT >= 8:
                            s_dv.wait(act)
                            s_ac.inc(act.activation(osq_sb[:, :], o_sb[:, :], AF.Square))
                            s_ac.wait(tsr)
                            s_dv.wait(tsr)
                            for g in range(NSG):
                                if g >= 1:
                                    s_dv.wait(tsr)
                                tsr.matmul(p_a[:, :], c_ho[:, :],
                                           o_sb[:, g * TILE:(g + 1) * TILE], start=True, stop=True)
                                ins = tsr.matmul(p_b[:, :], c_ho[:, :],
                                                 osq_sb[:, g * TILE:(g + 1) * TILE],
                                                 start=True, stop=True)
                                s_pe.inc(ins)
                                s_pe.wait(vec)
                                vec.tensor_scalar(gmu_sb[:, g * TILE:(g + 1) * TILE],
                                                  p_a[:, :], 1.0 / D, None, ALU.mult)
                                vec.tensor_mul(osq_sb[:, g * TILE:(g + 1) * TILE],
                                               gmu_sb[:, g * TILE:(g + 1) * TILE],
                                               gmu_sb[:, g * TILE:(g + 1) * TILE])
                                ins = vec.scalar_tensor_tensor(
                                    gs1_sb[:, g * TILE:(g + 1) * TILE],
                                    p_b[:, :], 1.0 / D, osq_sb[:, g * TILE:(g + 1) * TILE],
                                    ALU.mult, ALU.subtract)
                                s_dv.inc(ins)
                            s_dv.wait(act)
                            act.activation(gs1_sb[:, :], gs1_sb[:, :], AF.Ln, bias=c_eps[:, :])
                            s_ac.inc(act.activation(gs1_sb[:, :], gs1_sb[:, :], AF.Exp, scale=-0.5))
                            s_ac.wait(vec)
                            ins = None
                            for g in range(NSG):
                                vec.tensor_scalar(gs1_sb[:, g * TILE:(g + 1) * TILE],
                                                  gs1_sb[:, g * TILE:(g + 1) * TILE],
                                                  v_gnw[:, g:g + 1], None, ALU.mult)
                                vec.tensor_mul(osq_sb[:, g * TILE:(g + 1) * TILE],
                                               gmu_sb[:, g * TILE:(g + 1) * TILE],
                                               gs1_sb[:, g * TILE:(g + 1) * TILE])
                                vec.tensor_scalar(osq_sb[:, g * TILE:(g + 1) * TILE],
                                                  osq_sb[:, g * TILE:(g + 1) * TILE],
                                                  -1.0, v_gnb[:, g:g + 1], ALU.mult, ALU.add)
                                vec.tensor_mul(o_sb[:, g * TILE:(g + 1) * TILE],
                                               o_sb[:, g * TILE:(g + 1) * TILE],
                                               gs1_sb[:, g * TILE:(g + 1) * TILE])
                                ins = vec.tensor_add(o_sb[:, g * TILE:(g + 1) * TILE],
                                                     o_sb[:, g * TILE:(g + 1) * TILE],
                                                     osq_sb[:, g * TILE:(g + 1) * TILE])
                            s_dv.inc(ins)
                            ins = None
                            for g in range(NSG):
                                vec.tensor_mul(osq_sb[:, g * TILE:(g + 1) * TILE],
                                               r_sb[:, g * TILE:(g + 1) * TILE],
                                               kfin_sb[:, g * TILE:(g + 1) * TILE])
                                ins = vec.tensor_scalar(osq_sb[:, g * TILE:(g + 1) * TILE],
                                                        osq_sb[:, g * TILE:(g + 1) * TILE],
                                                        v_rk[:, g:g + 1], None, ALU.mult)
                            s_dv.inc(ins)
                            s_dv.wait(tsr)
                            s_ac.wait(vec)
                            for g in range(NSG):
                                if g >= 1:
                                    s_dv.wait(tsr)
                                ins = tsr.matmul(p_b[:, :], c_ho[:, :],
                                                 osq_sb[:, g * TILE:(g + 1) * TILE],
                                                 start=True, stop=True)
                                s_pe.inc(ins)
                                s_pe.wait(vec)
                                vec.tensor_mul(gmu_sb[:, g * TILE:(g + 1) * TILE],
                                               vv_sb[:, g * TILE:(g + 1) * TILE], p_b[:, :])
                                vec.tensor_add(o_sb[:, g * TILE:(g + 1) * TILE],
                                               o_sb[:, g * TILE:(g + 1) * TILE],
                                               gmu_sb[:, g * TILE:(g + 1) * TILE])
                                ins = vec.tensor_mul(og_sb[:, g * TILE:(g + 1) * TILE],
                                                     o_sb[:, g * TILE:(g + 1) * TILE],
                                                     g_sb[:, g * TILE:(g + 1) * TILE])
                                s_dv.inc(ins)
                            if ti == BUILD_TILES - 1:
                                tapdma("og", og_sb[:, :], 128, NSG * TILE, BF16)
                            s_dv.wait(syn)
                            s_st.dinc(syn.dma_start(
                                out=og_all.rearrange("b (p g n) -> b p g n", p=128, g=NSG)
                                [bg, :, :, (ti % 4) * TILE:(ti % 4) * TILE + TILE],
                                in_=og_sb[:].rearrange("p (g n) -> p g n", n=TILE)))
                            if ti % 4 == 3 and BUILD_PHASES >= 2:
                                s_st.wait(pool)
                                slot = 1 if bg < 4 else 0
                                brel = bg if bg < 4 else bg - 4
                                with pool.If_eq(role_rg, slot):
                                    pool.dma_start(
                                        out=sh_og[slot, brel:brel + 1, :],
                                        in_=og_all[bg:bg + 1, :]).then_inc(s_wt.h, 16)
                                    pool.wait_ge(s_wt.h, s_wt.n + 16)
                                    pool.dma_start(
                                        out=sh_flag[slot, brel:brel + 1, :],
                                        in_=v_nonce[brel:brel + 1, :]).then_inc(s_wt.h, 16)
                                with pool.Else():
                                    pool.dma_start(out=dummy_dram[:, :],
                                                   in_=v_nonce[brel:brel + 1, :]).then_inc(s_wt.h, 16)
                                    pool.wait_ge(s_wt.h, s_wt.n + 16)
                                    pool.dma_start(out=dummy_dram[:, :],
                                                   in_=v_nonce[brel:brel + 1, :]).then_inc(s_wt.h, 16)
                                pool.end_ifs()
                                s_wt.n += 32

                    s_b1.inc(tsr.nop())
                    s_b2.inc(vec.nop())
                    s_b3.inc(act.nop())
                    wait_all((s_b1, s_b2, s_b3, s_st, s_wt), (tsr, vec, act, syn, pool))

            if BUILD_PHASES >= 2:
                with contextlib.ExitStack() as _es:
                    woo_sb = _es.enter_context(nc.sbuf_tensor("woo_sb", [128, NSG * C], BF16))
                    wop_sb = _es.enter_context(nc.sbuf_tensor("wop_sb", [128, NSG * C], BF16))
                    ogo_sb = _es.enter_context(nc.sbuf_tensor("ogo_sb", [128, NSG * 1024], BF16))
                    ogp_sb = _es.enter_context(nc.sbuf_tensor("ogp_sb", [128, NSG * 1024], BF16))
                    xq_sb = _es.enter_context(nc.sbuf_tensor("xq_sb", [128, NCG * 256], F32))
                    x1_sb = _es.enter_context(nc.sbuf_tensor("x1_sb", [128, NCG * 256], F32))
                    flag_sb = _es.enter_context(nc.sbuf_tensor("flag_sb", [2, 64], I32))
                    wload(woo_sb[:], wo_own, C)
                    wload(wop_sb[:], wo_par, C)
                    ms = nc.monotonic_semaphore(0, engine=pool)
                    cnt = pool.alloc_register("pollcnt")
                    s_st.wait(pool)
                    for j in range(NBLK):
                        s_pe.wait(pool)
                        with pool.If_eq(role_rg, 0):
                            pool.dma_start(out=ogo_dram[:, :],
                                           in_=og_all[j:j + 1, :]).then_inc(s_wt.h, 16)
                        with pool.Else():
                            pool.dma_start(out=ogo_dram[:, :],
                                           in_=og_all[4 + j:5 + j, :]).then_inc(s_wt.h, 16)
                        pool.end_ifs()
                        s_wt.n += 16
                        s_wt.wait(pool)
                        ins = pool.dma_start(
                            out=ogo_sb[:].rearrange("p (g t) -> p g t", t=1024),
                            in_=ogo_dram.rearrange("o (p g n) -> p (o g) n", p=128, g=NSG))
                        s_wt.dinc(ins)
                        nv = pool.value_load(v_nonce[j:j + 1, 0:1])
                        pool.reg_mov(cnt, 0)

                        def cond_poll(jj=j):
                            ms.inc_expected(32)
                            pool.dma_start(out=flag_sb[0:1, 0:64],
                                           in_=sh_flag[0, jj:jj + 1, :]).then_inc(ms.sem(), 16)
                            pool.dma_start(out=flag_sb[1:2, 0:64],
                                           in_=sh_flag[1, jj:jj + 1, :]).then_inc(ms.sem(), 16)
                            ms.wait()
                            v0 = pool.value_load(flag_sb[0:1, 0:1])
                            v1 = pool.value_load(flag_sb[1:2, 0:1])
                            pool.reg_add(cnt, cnt, 1)
                            cc = make_scalar_value(cnt, min_val=0, max_val=None)
                            from concourse.expressions import s_logical_or
                            return s_logical_and(
                                s_logical_or(s_not_equal(v0, nv), s_not_equal(v1, nv)),
                                cc < 3000000)

                        with pool.While(cond_poll):
                            pool.engine_nop()
                        with pool.If_eq(role_rg, 0):
                            pool.dma_start(out=ogp_dram[:, :],
                                           in_=sh_og[1, j:j + 1, :]).then_inc(s_wt.h, 16)
                        with pool.Else():
                            pool.dma_start(out=ogp_dram[:, :],
                                           in_=sh_og[0, j:j + 1, :]).then_inc(s_wt.h, 16)
                        pool.end_ifs()
                        s_wt.n += 16
                        s_wt.wait(pool)
                        ins = pool.dma_start(
                            out=ogp_sb[:].rearrange("p (g t) -> p g t", t=1024),
                            in_=ogp_dram.rearrange("o (p g n) -> p (o g) n", p=128, g=NSG))
                        s_wt.dinc(ins)
                        s_wt.wait(syn)
                        s_wt.wait(tsr)
                        for stt in range(4):
                            tof = j * 1024 + stt * 256
                            s_xt.dinc(syn.dma_start(
                                out=xq_sb[:].rearrange("p (g n) -> p g n", n=256),
                                in_=x_own[:, tof:tof + 256].rearrange("(g p) n -> p g n", p=128)))
                            s_xt.wait(tsr)
                            s_xt.wait(vec)
                            s_st.wait(vec)
                            with contextlib.ExitStack() as _es:
                                p2a = _es.enter_context(nc.psum_tensor(f"p2a_{j}_{stt}", [128, 256], F32))
                                p2b = _es.enter_context(nc.psum_tensor(f"p2b_{j}_{stt}", [128, 256], F32))
                                for mc in range(NCG):
                                    pp = p2a if mc % 2 == 0 else p2b
                                    if mc >= 2:
                                        s_dv.wait(tsr)
                                    for kc in range(NCG):
                                        wsb = woo_sb if kc < NSG else wop_sb
                                        osb = ogo_sb if kc < NSG else ogp_sb
                                        kcl = kc % NSG
                                        ins = tsr.matmul(
                                            pp[:, :],
                                            wsb[:, kcl * C + mc * 128:kcl * C + mc * 128 + 128],
                                            osb[:, kcl * 1024 + stt * 256:kcl * 1024 + stt * 256 + 256],
                                            start=(kc == 0), stop=(kc == NCG - 1))
                                    s_pe.inc(ins)
                                    s_pe.wait(vec)
                                    ins = vec.tensor_tensor(
                                        x1_sb[:, mc * 256:(mc + 1) * 256], pp[:, :],
                                        xq_sb[:, mc * 256:(mc + 1) * 256], ALU.add)
                                    s_dv.inc(ins)
                                s_dv.wait(syn)
                                s_st.dinc(syn.dma_start(
                                    out=x1_sp[:, :].rearrange("p (g t) -> p g t", t=TOK_OWN)
                                    [:, :, tof:tof + 256],
                                    in_=x1_sb[:].rearrange("p (g n) -> p g n", n=256)))
                    if "x1" in DEBUG_TAPS:
                        tapdma("x1", x1_sb[:, :], 128, NCG * 256)
                    s_st.wait(syn)

        if BUILD_PHASES >= 3:
            with contextlib.ExitStack() as _es:
                acc_sb = _es.enter_context(nc.sbuf_tensor("acc_sb", [128, NCG * FSPAN], F32))
                x2_sb = _es.enter_context(nc.sbuf_tensor("x2_sb", [128, NCG * FSPAN], BF16))
                f32s_sb = _es.enter_context(nc.sbuf_tensor("f32s_sb", [128, NCG * FSPAN], F32))
                fmu_sb = _es.enter_context(nc.sbuf_tensor("fmu_sb", [128, FSPAN], F32))
                fvar_sb = _es.enter_context(nc.sbuf_tensor("fvar_sb", [128, FSPAN], F32))
                w1a_sb = _es.enter_context(nc.sbuf_tensor("w1a_sb", [128, NCG * 2048], BF16))
                w1b_sb = _es.enter_context(nc.sbuf_tensor("w1b_sb", [128, NCG * 2048], BF16))
                w2_sb = _es.enter_context(nc.sbuf_tensor("w2_sb", [128, 2 * 4 * C], BF16))
                b1_sb = _es.enter_context(nc.sbuf_tensor("b1_sb", [1, FF], BF16))
                b2_sb = _es.enter_context(nc.sbuf_tensor("b2_sb", [1, C], BF16))
                h_sb = _es.enter_context(nc.sbuf_tensor("h_sb", [128, 32 * 256], BF16))
                gt_sb = _es.enter_context(nc.sbuf_tensor("gt_sb", [128, 4 * 9], F32))
                zrow_sb = _es.enter_context(nc.sbuf_tensor("zrow_sb", [1, 512], BF16))
                wn_sb = _es.enter_context(nc.sbuf_tensor("wn_sb", [128, 128], F32))
                wT_sb = _es.enter_context(nc.sbuf_tensor("wT_sb", [8, FSPAN], F32))
                wbc_sb = _es.enter_context(nc.sbuf_tensor("wbc_sb", [128, 256], F32))
                rs_sb = _es.enter_context(nc.sbuf_tensor("rs_sb", [128, 4], F32))
                ri_sb = _es.enter_context(nc.sbuf_tensor("ri_sb", [128, 4], F32))
                thr_sb = _es.enter_context(nc.sbuf_tensor("thr_sb", [128, 4], F32))
                c_tiny = _es.enter_context(nc.sbuf_tensor("c_tiny", [128, 1], F32))
                s_in.wait(vec)
                vec.memset(zrow_sb[:, :], 0.0)
                vec.memset(c_tiny[:, :], 1e-38)
                s_dv.inc(vec.memset(wn_sb[:, :], 0.0))
                for sp in range(NSPAN):
                    sp0 = sp * FSPAN
                    with contextlib.ExitStack() as _es:
                        pf_a = _es.enter_context(nc.psum_tensor(f"pf_a_{sp}", [128, 256], F32))
                        pf_b = _es.enter_context(nc.psum_tensor(f"pf_b_{sp}", [128, 256], F32))
                        s_dv.wait(syn)
                        s_pe.wait(syn)
                        s_st.wait(syn)
                        s_xt.dinc(syn.dma_start(
                            out=acc_sb[:].rearrange("p (g n) -> p g n", n=FSPAN),
                            in_=x1_sp[:, :].rearrange("p (g t) -> p g t", t=TOK_OWN)
                            [:, :, sp0:sp0 + FSPAN]))
                        wait_all((s_xt,), (tsr, vec, act))
                        s_ac.inc(act.activation(f32s_sb[:, :], acc_sb[:, :], AF.Square))
                        s_ac.wait(tsr)
                        for stt in range(2):
                            if stt > 0:
                                s_dv.wait(tsr)
                            for g in range(NCG):
                                tsr.matmul(pf_a[:, :], c_ones[:, :],
                                           acc_sb[:].rearrange("p (g n) -> p g n", n=FSPAN)
                                           [:, g, stt * 256:stt * 256 + 256],
                                           start=(g == 0), stop=(g == NCG - 1))
                            for g in range(NCG):
                                ins = tsr.matmul(pf_b[:, :], c_ones[:, :],
                                                 f32s_sb[:].rearrange("p (g n) -> p g n", n=FSPAN)
                                                 [:, g, stt * 256:stt * 256 + 256],
                                                 start=(g == 0), stop=(g == NCG - 1))
                            s_pe.inc(ins)
                            s_pe.wait(vec)
                            vec.tensor_scalar(fmu_sb[:, stt * 256:stt * 256 + 256],
                                              pf_a[:, :], 1.0 / C, None, ALU.mult)
                            vec.tensor_mul(fvar_sb[:, stt * 256:stt * 256 + 256],
                                           fmu_sb[:, stt * 256:stt * 256 + 256],
                                           fmu_sb[:, stt * 256:stt * 256 + 256])
                            ins = vec.scalar_tensor_tensor(
                                fvar_sb[:, stt * 256:stt * 256 + 256],
                                pf_b[:, :], 1.0 / C,
                                fvar_sb[:, stt * 256:stt * 256 + 256],
                                ALU.mult, ALU.subtract)
                            s_dv.inc(ins)
                        s_dv.wait(act)
                        act.activation(fvar_sb[:, :], fvar_sb[:, :], AF.Ln, bias=c_eps[:, :])
                        s_ac.inc(act.activation(fvar_sb[:, :], fvar_sb[:, :], AF.Exp, scale=-0.5))
                        s_ac.wait(vec)
                        s_dv.inc(vec.scalar_tensor_tensor(
                            fmu_sb[:, :], fmu_sb[:, :], -1.0, fvar_sb[:, :],
                            ALU.mult, ALU.mult))
                        ins = None
                        for g in range(NCG):
                            vec.tensor_tensor(f32s_sb[:, g * FSPAN:(g + 1) * FSPAN],
                                              acc_sb[:, g * FSPAN:(g + 1) * FSPAN],
                                              fvar_sb[:, :], ALU.mult)
                            ins = vec.tensor_tensor(x2_sb[:, g * FSPAN:(g + 1) * FSPAN],
                                                    f32s_sb[:, g * FSPAN:(g + 1) * FSPAN],
                                                    fmu_sb[:, :], ALU.add)
                        s_dv.inc(ins)
                        if sp == 0:
                            tapdma("x2", x2_sb[:, :], 128, NCG * FSPAN, BF16)
                        if FFN_LIMIT >= 2:
                            s_dv.wait(tsr)
                            for blk in range(4):
                                for kc in range(NCG):
                                    ins = tsr.matmul(
                                        pf_a[:, blk * 9:blk * 9 + 9],
                                        x2_sb[:, kc * FSPAN + blk * 128:kc * FSPAN + blk * 128 + 128],
                                        wgt_sb[:, kc * 9:(kc + 1) * 9],
                                        start=(kc == 0), stop=(kc == NCG - 1))
                                s_pe.inc(ins)
                            s_pe.wait(vec)
                            vec.tensor_copy(gt_sb[:, :], pf_a[:, 0:36])
                            ins = None
                            for blk in range(4):
                                ins = vec.tensor_add(gt_sb[:, blk * 9:blk * 9 + 8],
                                                     gt_sb[:, blk * 9:blk * 9 + 8], v_bg[:, :])
                            s_dv.inc(ins)
                            if sp == 0:
                                tapdma("gtraw", gt_sb[:, :], 128, 36)
                            s_dv.wait(act)
                            for blk in range(4):
                                act.activation(gt_sb[:, blk * 9:blk * 9 + 8],
                                               gt_sb[:, blk * 9:blk * 9 + 8], AF.Exp)
                            ins = None
                            for blk in range(4):
                                ins = act.activation(thr_sb[:, blk:blk + 1],
                                                     gt_sb[:, blk * 9 + 8:blk * 9 + 9],
                                                     AF.Tanh, bias=v_bth[:, :], scale=0.5)
                            s_ac.inc(ins)
                            s_ac.wait(vec)
                            for blk in range(4):
                                vec.reduce_sum(rs_sb[:, blk:blk + 1],
                                               gt_sb[:, blk * 9:blk * 9 + 8],
                                               axis=mybir.AxisListType.X)
                            ins = vec.tensor_scalar(thr_sb[:, 0:4], thr_sb[:, 0:4],
                                                    0.0625, 0.0625, ALU.mult, ALU.add)
                            s_dv.inc(ins)
                            s_dv.wait(act)
                            act.activation(ri_sb[:, 0:4], rs_sb[:, 0:4], AF.Ln)
                            ins = act.activation(ri_sb[:, 0:4], ri_sb[:, 0:4], AF.Exp, scale=-1.0)
                            s_ac.inc(ins)
                            s_ac.wait(vec)
                            for blk in range(4):
                                gslice = gt_sb[:, blk * 9:blk * 9 + 8]
                                vec.tensor_scalar(gslice, gslice, ri_sb[:, blk:blk + 1],
                                                  None, ALU.mult)
                            for blk in range(4):
                                gslice = gt_sb[:, blk * 9:blk * 9 + 8]
                                vec.tensor_scalar(gslice, gslice, thr_sb[:, blk:blk + 1],
                                                  None, ALU.subtract)
                            for blk in range(4):
                                gslice = gt_sb[:, blk * 9:blk * 9 + 8]
                                vec.tensor_scalar_max(gslice, gslice, 0.0)
                            ins = None
                            for blk in range(4):
                                ins = vec.reduce_sum(rs_sb[:, blk:blk + 1],
                                                     gt_sb[:, blk * 9:blk * 9 + 8],
                                                     axis=mybir.AxisListType.X)
                            s_dv.inc(ins)
                            s_dv.wait(act)
                            act.activation(ri_sb[:, 0:4], rs_sb[:, 0:4], AF.Ln,
                                           bias=c_tiny[:, :])
                            ins = act.activation(ri_sb[:, 0:4], ri_sb[:, 0:4], AF.Exp, scale=-1.0)
                            s_ac.inc(ins)
                            s_ac.wait(vec)
                            ins = None
                            for blk in range(4):
                                ins = vec.tensor_scalar(wn_sb[:, blk * 32:blk * 32 + 8],
                                                        gt_sb[:, blk * 9:blk * 9 + 8],
                                                        ri_sb[:, blk:blk + 1], None, ALU.mult)
                            s_dv.inc(ins)
                            if sp == 0:
                                tapdma("gt2", gt_sb[:, :], 128, 36)
                                tapdma("wn", wn_sb[:, :], 128, 128)
                        if FFN_LIMIT >= 3:
                            s_dv.wait(tsr)
                            ins = tsr.matmul(pf_b[:, 0:128], wn_sb[:, :], c_idf[:, :],
                                             start=True, stop=True)
                            s_pe.inc(ins)
                            s_pe.wait(vec)
                            ins = None
                            for blk in range(4):
                                ins = vec.tensor_copy(wT_sb[:, blk * 128:blk * 128 + 128],
                                                      pf_b[blk * 32:blk * 32 + 8, 0:128])
                            s_dv.inc(ins)
                            if sp == 0:
                                tapdma("wT", wT_sb[:, :], 8, FSPAN)
                        if FFN_LIMIT >= 4:
                            for e in range(NE):
                                wbuf_lo, wbuf_hi = w1a_sb, w1b_sb
                                s_pe.wait(pool)
                                s_wt.dinc(pool.dma_start(
                                    out=wbuf_lo[:].rearrange("p (k w) -> p k w", w=2048),
                                    in_=ffw1[e, :, 0:2048].rearrange("(k p) w -> p k w", p=128)))
                                s_wt.dinc(pool.dma_start(
                                    out=wbuf_hi[:].rearrange("p (k w) -> p k w", w=2048),
                                    in_=ffw1[e, :, 2048:4096].rearrange("(k p) w -> p k w", p=128)))
                                s_wt.dinc(pool.dma_start(out=b1_sb[:, :], in_=ffb1[e:e + 1, :]))
                                s_wt.dinc(pool.dma_start(out=b2_sb[:, :], in_=ffb2[e:e + 1, :]))
                                s_wt.wait(tsr)
                                for tt in range(2):
                                    ttof = tt * 256
                                    s_dv.wait(tsr)
                                    ins = tsr.matmul(pf_b[:, :],
                                                     c_sel[:, e * 128:(e + 1) * 128],
                                                     wT_sb[:, ttof:ttof + 256],
                                                     start=True, stop=True)
                                    s_pe.inc(ins)
                                    s_pe.wait(vec)
                                    s_dv.inc(vec.tensor_copy(wbc_sb[:, :], pf_b[:, :]))
                                    with contextlib.ExitStack() as _es:
                                        ph0 = _es.enter_context(nc.psum_tensor(f"ph0_{sp}_{e}_{tt}", [128, 512], F32))
                                        ph1 = _es.enter_context(nc.psum_tensor(f"ph1_{sp}_{e}_{tt}", [128, 512], F32))
                                        pw2 = _es.enter_context(nc.psum_tensor(f"pw2_{sp}_{e}_{tt}", [128, 2048], F32))
                                        s_dv.wait(tsr)
                                        s_ac.wait(tsr)
                                        for bb in range(4):
                                            ins = tsr.matmul(pw2[:, bb * 512:bb * 512 + 512],
                                                             c_1row[0:1, 0:128], zrow_sb[:, :],
                                                             start=True, stop=False)
                                        s_pe.inc(ins)
                                        for f2 in range(16):
                                            ph = ph0 if f2 % 2 == 0 else ph1
                                            if f2 >= 2:
                                                s_ac.wait(tsr)
                                            for sub in range(2):
                                                f = f2 * 2 + sub
                                                wb = wbuf_lo if f < 16 else wbuf_hi
                                                fl = f % 16
                                                for kc in range(NCG):
                                                    tsr.matmul(
                                                        ph[:, sub * 256:sub * 256 + 256],
                                                        wb[:, kc * 2048 + fl * 128:kc * 2048 + fl * 128 + 128],
                                                        x2_sb[:, kc * FSPAN + ttof:kc * FSPAN + ttof + 256],
                                                        start=(kc == 0), stop=False)
                                                ins = tsr.matmul(
                                                    ph[:, sub * 256:sub * 256 + 256],
                                                    b1_sb[:, f * 128:f * 128 + 128],
                                                    c_1row[:, 0:256], start=False, stop=True)
                                            s_pe.inc(ins)
                                            s_pe.wait(act)
                                            ins = act.activation(
                                                h_sb[:, f2 * 512:f2 * 512 + 512],
                                                ph[:, :], AF.Gelu_apprx_tanh)
                                            s_ac.inc(ins)
                                        if FFN_LIMIT >= 5:
                                            s_ac.wait(tsr)
                                            s_ac.wait(pool)
                                            for fq in range(8):
                                                s_pe.wait(pool)
                                                s_wt.dinc(pool.dma_start(
                                                    out=w2_sb[:, (fq % 2) * 4 * C:(fq % 2) * 4 * C + 4 * C]
                                                    .rearrange("p (f w) -> p f w", w=C),
                                                    in_=ffw2[e, fq * 512:(fq + 1) * 512, :]
                                                    .rearrange("(f p) w -> p f w", p=128)))
                                                s_wt.wait(tsr)
                                                for ff in range(4):
                                                    f = fq * 4 + ff
                                                    for dc in range(NCG):
                                                        ins = tsr.matmul(
                                                            pw2[:, dc * 256:dc * 256 + 256],
                                                            w2_sb[:, (fq % 2) * 4 * C + ff * C + dc * 128:
                                                                  (fq % 2) * 4 * C + ff * C + dc * 128 + 128],
                                                            h_sb[:, f * 256:f * 256 + 256],
                                                            start=False, stop=False)
                                                s_pe.inc(ins)
                                            for dc in range(NCG):
                                                tsr.matmul(pw2[:, dc * 256:dc * 256 + 256],
                                                           b2_sb[:, dc * 128:dc * 128 + 128],
                                                           c_1row[:, 0:256], start=False, stop=False)
                                            for bb in range(4):
                                                ins = tsr.matmul(pw2[:, bb * 512:bb * 512 + 512],
                                                                 c_1row[0:1, 0:128], zrow_sb[:, :],
                                                                 start=False, stop=True)
                                            s_pe.inc(ins)
                                            s_pe.wait(vec)
                                            ins = None
                                            for dc in range(NCG):
                                                vec.tensor_tensor(
                                                    f32s_sb[:, dc * FSPAN + ttof:dc * FSPAN + ttof + 256],
                                                    pw2[:, dc * 256:dc * 256 + 256],
                                                    wbc_sb[:, :], ALU.mult)
                                                ins = vec.tensor_add(
                                                    acc_sb[:, dc * FSPAN + ttof:dc * FSPAN + ttof + 256],
                                                    acc_sb[:, dc * FSPAN + ttof:dc * FSPAN + ttof + 256],
                                                    f32s_sb[:, dc * FSPAN + ttof:dc * FSPAN + ttof + 256])
                                            s_dv.inc(ins)
                        s_dv.wait(syn)
                        s_st.dinc(syn.dma_start(
                            out=y_out[:, sp0:sp0 + FSPAN].rearrange("(g p) n -> p g n", p=128),
                            in_=acc_sb[:].rearrange("p (g n) -> p g n", n=FSPAN)))
        else:
            with nc.sbuf_tensor("zf", [128, 512], F32) as zf:
                vec.memset(zf[:, :], 0.0)
                s_dv.inc(vec.tensor_copy(zf[:, 0:1], zf[:, 1:2]))
                s_dv.wait(syn)
                for g in range(NCG):
                    for jj in range(TOK_OWN // 512):
                        s_st.dinc(syn.dma_start(
                            out=y_out[g * 128:(g + 1) * 128, jj * 512:(jj + 1) * 512],
                            in_=zf[:, :]))
        s_st.wait(syn)
        wait_all((s_st,), (tsr, vec, act, pool))
    return nc


def _host_inputs(inputs, core):
    b, hh = core // 2, core % 2
    own = slice(512 * hh, 512 * hh + 512)
    par = slice(512 * (1 - hh), 512 * (1 - hh) + 512)
    perm = np.r_[np.arange(512 * hh, 512 * hh + 512),
                 np.arange(512 * (1 - hh), 512 * (1 - hh) + 512)]
    f32 = np.float32

    def col128(v):
        return np.ascontiguousarray(np.asarray(v, f32).reshape(-1, 128).T)

    x = np.asarray(inputs["L0_attn_hidden_states"][b], f32)
    xmix = np.stack([np.asarray(inputs["L0_attn_x_" + s], f32) for s in "rwkva"])
    xmix = np.ascontiguousarray(xmix.reshape(5, 8, 128).transpose(2, 0, 1).reshape(128, 40))
    m = {
        "x_cm": np.ascontiguousarray(x.T),
        "x_own": np.ascontiguousarray(x[4096 * hh:4096 * hh + 4096].T),
        "w_r": np.ascontiguousarray(np.asarray(inputs["L0_attn_W_r"], f32)[:, perm]),
        "w_k": np.ascontiguousarray(np.asarray(inputs["L0_attn_W_k"], f32)[:, own]),
        "w_v": np.ascontiguousarray(np.asarray(inputs["L0_attn_W_v"], f32)[:, own]),
        "wo_own": np.ascontiguousarray(np.asarray(inputs["L0_attn_W_o"], f32)[own, :]),
        "wo_par": np.ascontiguousarray(np.asarray(inputs["L0_attn_W_o"], f32)[par, :]),
        "wla": np.asarray(inputs["L0_attn_w_lora_a"], f32),
        "wlb": np.ascontiguousarray(np.asarray(inputs["L0_attn_w_lora_b"], f32)[:, own]),
        "wlbias": col128(np.asarray(inputs["L0_attn_w_lora_bias"], f32)[own]),
        "ala": np.asarray(inputs["L0_attn_a_lora_a"], f32),
        "alb": np.ascontiguousarray(np.asarray(inputs["L0_attn_a_lora_b"], f32)[:, own]),
        "albias": col128(np.asarray(inputs["L0_attn_a_lora_bias"], f32)[own]),
        "gla": np.ascontiguousarray(np.asarray(inputs["L0_attn_g_lora_a"], f32)[perm, :]),
        "glb": np.ascontiguousarray(np.asarray(inputs["L0_attn_g_lora_b"], f32)[:, own]),
        "gbias": col128(np.asarray(inputs["L0_attn_g_lora_bias"], f32)[own]),
        "xmix": xmix,
        "kkv": col128(np.asarray(inputs["L0_attn_k_k"], f32)[own]),
        "kav": col128(np.asarray(inputs["L0_attn_k_a"], f32)[own]),
        "rkv": col128(np.asarray(inputs["L0_attn_r_k"], f32).reshape(-1)[own]),
        "gnw": col128(np.asarray(inputs["L0_attn_gn_weight"], f32)[own]),
        "gnb": col128(np.asarray(inputs["L0_attn_gn_bias"], f32)[own]),
        "wgt": np.ascontiguousarray(np.concatenate(
            [np.asarray(inputs["L0_ffn_Wg"], f32),
             np.asarray(inputs["L0_ffn_Wt"], f32)], axis=1)),
        "btv": np.full((128, 1), np.asarray(inputs["L0_ffn_bt"], f32)[0], f32),
        "bgv": np.tile(np.asarray(inputs["L0_ffn_bg"], f32)[None, :], (128, 1)),
        "ffw1": np.asarray(inputs["L0_ffn_W1"], f32),
        "ffb1": np.asarray(inputs["L0_ffn_b1"], f32),
        "ffw2": np.asarray(inputs["L0_ffn_W2"], f32),
        "ffb2": np.asarray(inputs["L0_ffn_b2"], f32),
        "ones128": np.ones((128, 128), f32),
        "headones": np.kron(np.eye(2, dtype=f32), np.ones((64, 64), f32)),
        "ident": np.eye(128, dtype=ml_dtypes.bfloat16),
        "identf": np.eye(128, dtype=f32),
        "m_sl": np.tile(np.kron(np.eye(2, dtype=f32), np.tril(np.ones((32, 32), f32), -1)), (2, 4)),
        "m_su": np.tile(np.kron(np.eye(2, dtype=f32), np.triu(np.ones((32, 32), f32), 1)), (2, 4)),
        "m_ui": np.tile(np.kron(np.eye(2, dtype=f32), np.triu(np.ones((32, 32), f32), 0)), (2, 4)),
        "i64rep": np.tile(np.eye(64), (2, 4)).astype(ml_dtypes.bfloat16),
        "onesrow": np.ones((1, 512), ml_dtypes.bfloat16),
        "sel_e": np.concatenate([np.tile(np.eye(8, dtype=f32)[:, e:e + 1], (1, 128))
                                 for e in range(8)], axis=1),
        "epsc": np.full((128, 1), EPS, f32),
    }
    return m


_NONCE_CTR = [int(time.time_ns()) & 0x3FFFFFF]


def kernel(**inputs):
    _NONCE_CTR[0] += 17
    base = (_NONCE_CTR[0] * 2654435761) & 0x3FFFFFFF
    nonces = np.zeros((NBLK, 64), np.int32)
    for j in range(NBLK):
        nonces[j, :] = np.int32((base + j * 1000003) & 0x3FFFFFFF)

    in_maps = []
    for c in range(8):
        m = _host_inputs(inputs, c)
        m["nonces"] = nonces
        in_maps.append(m)

    nc = build_program()
    res = run_bass_kernel_spmd(nc, in_maps, core_ids=list(range(8)))
    out = np.zeros((B, T, C), np.float32)
    for c in range(8):
        b, hh = c // 2, c % 2
        y = res.results[c]["y_out"]
        out[b, 4096 * hh:4096 * hh + 4096, :] = np.asarray(y, np.float32).T
    kernel.last_results = res
    return out
